# version 29 (final)
# baseline (speedup 1.0000x reference)
import numpy as np
import ml_dtypes
import concourse.bass as bass
import concourse.mybir as mybir
from concourse.bass_utils import run_bass_kernel_spmd

F32 = mybir.dt.float32
BF16 = mybir.dt.bfloat16
AF = mybir.ActivationFunctionType
ALU = mybir.AluOpType
bfloat16 = ml_dtypes.bfloat16

ENGINES = ("pe", "act", "dve", "pool", "sp")


class Cfg:
    def __init__(self, S_total=8192, D=2048, HF=8192, NH=16, n_cores=8):
        self.n_cores = n_cores
        self.D = D
        self.HF = HF
        self.NH = NH
        self.DH = D // NH
        self.SP = S_total // n_cores
        self.SB = S_total // 2
        self.NG = 4
        self.GR = self.SB // self.NG
        self.NT = self.SP // 128
        self.NTG = self.GR // 128
        self.NJ = D // 128
        self.NGH = HF // 128
        self.TAU = self.SB // 128
        self.NS = self.SP // 128
        self.NCH = min(512, self.SP)
        self.NHALF = self.SP // self.NCH
        self.GCH = min(512, self.GR)
        self.GHALF = self.GR // self.GCH
        self.TC = min(512, self.SP)
        self.NTC = self.SP // self.TC
        self.TCSUB = self.TC // 128
        self.EXP_SHIFT = -12.0
        self.EXP_SCALE = float(1.0 / np.sqrt(self.DH))



class Op:
    __slots__ = ("eng", "fn", "reads", "writes", "dma_slot", "tag",
                 "waits", "incs")

    def __init__(self, eng, fn, reads, writes, dma_slot, tag):
        self.eng = eng
        self.fn = fn
        self.reads = reads
        self.writes = writes
        self.dma_slot = dma_slot
        self.tag = tag
        self.waits = []
        self.incs = []


class Tracker:
    def __init__(self, alias_of):
        self.alias_of = alias_of
        self.ops = []
        self.seq = {e: 0 for e in ENGINES}
        self.last_w = {}
        self.readers = {}
        self.synced = {e: {} for e in ENGINES}
        self.dma_cum = {}
        self.w_idx = {}

    def chain(self, eng):
        return "ch_" + eng

    def slot_sem(self, slot):
        name = "ds_" + slot
        if name not in self.dma_cum:
            self.dma_cum[name] = 0
        return name

    def _maybe_wait(self, op, eng, dep):
        if dep[0] == "E":
            _, e2, s2 = dep
            if e2 == "pe" and eng == "pe":
                return
            sem = self.chain(e2)
            if self.synced[eng].get(sem, 0) >= s2:
                return
            op.waits.append((sem, s2))
            self.synced[eng][sem] = s2
        else:
            _, sem, cum = dep
            if self.synced[eng].get(sem, 0) >= cum:
                return
            op.waits.append((sem, cum))
            self.synced[eng][sem] = cum

    def add(self, eng, fn, reads=(), writes=(), dma_slot=None, tag="",
            dma_step=16):
        op = Op(eng, fn, list(reads), list(writes), dma_slot, tag)
        deps = {}

        def push(d):
            k = (d[0], d[1])
            if k not in deps or deps[k][2] < d[2]:
                deps[k] = d

        for k in op.reads:
            d = self.last_w.get(k)
            if d is not None:
                push(d)
            own = self.w_idx.get(k, -1)
            for a in self.alias_of.get(k, ()):
                if self.w_idx.get(a, -1) > own:
                    raise AssertionError(
                        f"op {tag!r} reads {k} but alias {a} was written more "
                        f"recently (stale/clobbered data)")
        wkeys = set(op.writes)
        for k in op.writes:
            for a in self.alias_of.get(k, ()):
                wkeys.add(a)
        for k in wkeys:
            d = self.last_w.get(k)
            if d is not None:
                push(d)
            for rk, rv in self.readers.get(k, {}).items():
                push((rk[0], rk[1], rv))
        for d in deps.values():
            self._maybe_wait(op, eng, d)

        if dma_slot is not None:
            sem = self.slot_sem(dma_slot)
            self.dma_cum[sem] += dma_step if dma_step is not None else 1
            op.incs.append((sem, dma_step))
            wtok = ("S", sem, self.dma_cum[sem])
        else:
            self.seq[eng] += 1
            op.incs.append((self.chain(eng), 1))
            wtok = ("E", eng, self.seq[eng])

        for k in op.writes:
            self.last_w[k] = wtok
            self.w_idx[k] = len(self.ops)
            self.readers[k] = {}
        for k in op.reads:
            self.readers.setdefault(k, {})
            self.readers[k][(wtok[0], wtok[1])] = wtok[2]
        self.ops.append(op)
        return op



def build_program(c):
    nc = bass.Bass("TRN2", target_bir_lowering=False)

    x_own = nc.dram_tensor("x_all", [c.SP, c.D], F32, kind="ExternalInput").ap()
    wqT = nc.dram_tensor("wqT", [c.D, c.D], F32, kind="ExternalInput").ap()
    wkT = nc.dram_tensor("wkT", [c.D, c.D], F32, kind="ExternalInput").ap()
    wvT = nc.dram_tensor("wvT", [c.D, c.D], F32, kind="ExternalInput").ap()
    woT = nc.dram_tensor("woT", [c.D, c.D], F32, kind="ExternalInput").ap()
    w1T = nc.dram_tensor("w1T", [c.D, c.HF], F32, kind="ExternalInput").ap()
    w3T = nc.dram_tensor("w3T", [c.D, c.HF], F32, kind="ExternalInput").ap()
    w2T = nc.dram_tensor("w2T", [c.HF, c.D], F32, kind="ExternalInput").ap()
    biases = nc.dram_tensor("biases", [4, c.D], F32, kind="ExternalInput").ap()
    consts = nc.dram_tensor("consts", [128, 256], BF16, kind="ExternalInput").ap()
    out = nc.dram_tensor("out", [c.SP, c.D], F32, kind="ExternalOutput").ap()

    kv_ok = nc.dram_tensor("kv_ok", [c.D, c.SP], BF16).ap()
    kv_ov = nc.dram_tensor("kv_ov", [c.NH * c.SP, c.DH], BF16).ap()
    kv_agk = nc.dram_tensor("kv_agk", [4 * c.D, c.SP], BF16).ap()
    kv_agv = nc.dram_tensor("kv_agv", [4 * c.NH * c.SP, c.DH], BF16).ap()
    y_dram = nc.dram_tensor("y_dram", [c.SP, c.D], BF16).ap()

    sb_const = nc.alloc_sbuf_tensor("sb_const", [128, 8], F32)
    nc.const_aps.register(F32, 0.0, sb_const[:, 0:1])
    nc.const_aps.register(F32, 1.0, sb_const[:, 1:2])
    nc.const_aps.register(F32, c.EXP_SHIFT, sb_const[:, 2:3])
    nc.const_aps.register(F32, c.EXP_SCALE, sb_const[:, 3:4])
    sb_ones = nc.alloc_sbuf_tensor("sb_ones", [1, 128], BF16)
    sb_ident = nc.alloc_sbuf_tensor("sb_ident", [128, 128], BF16)
    sb_h128 = nc.alloc_sbuf_tensor("sb_h128", [128, 128], BF16)
    sb_bqcol = nc.alloc_sbuf_tensor("sb_bqcol", [128, c.NJ], F32)
    sb_bkcol = nc.alloc_sbuf_tensor("sb_bkcol", [128, c.NJ], F32)
    sb_bvb = nc.alloc_sbuf_tensor("sb_bvb", [128, c.D], BF16)
    sb_bob = nc.alloc_sbuf_tensor("sb_bob", [128, c.D], BF16)
    sb_st = nc.alloc_sbuf_tensor("sb_st", [128, 80], F32)

    KB = 1024
    BASE = 28 * KB
    manual = {}

    def at(name, shape, dtype, off_kb_x2=None, off_bytes=None):
        off = off_bytes if off_bytes is not None else 0
        off = (off + 31) // 32 * 32
        h = nc.alloc_sbuf_tensor_at(name, shape, dtype, offset=BASE + off)
        nbytes = int(np.prod(shape[1:])) * (2 if dtype == BF16 else 4)
        manual[name] = (off, off + nbytes)
        return h

    SPB = c.SP * 2
    m_xg = [at(f"xg{i}", [128, c.D], BF16, off_bytes=i * 4 * KB) for i in range(2)]
    m_xn = [at(f"xn{i}", [128, c.D], BF16, off_bytes=8 * KB + i * 4 * KB) for i in range(6)]
    m_kts = [at(f"kts{i}", [128, c.GR], BF16, off_bytes=96 * KB + i * 2 * c.GR) for i in range(2)]
    m_vs = [at(f"vs{i}", [128, 512], BF16, off_bytes=100 * KB + i * KB) for i in range(4)]
    m_btmp = at("btmp", [1, c.D], BF16, off_bytes=104 * KB)
    VBB = (c.TAU * (c.DH + 1) * 2 + 31) // 32 * 32
    m_vb = [at(f"vb{i}", [128, c.TAU * (c.DH + 1)], BF16, off_bytes=i * VBB) for i in range(2)]
    m_pt = [at(f"pt{i}", [128, c.SP], BF16, off_bytes=2 * VBB + i * SPB) for i in range(2)]
    m_lntmp = at("lntmp", [128, c.D], F32, off_bytes=0)
    m_xrb = [at(f"xrb{i}", [128, c.D], BF16, off_bytes=8 * KB + i * 4 * KB) for i in range(2)]
    m_ystg = at("ystg", [128, c.D], BF16, off_bytes=16 * KB)
    m_sil = [at(f"sil{i}", [128, c.TC], BF16, off_bytes=i * c.TC * 2) for i in range(2)]
    m_hpre = [at(f"hpre{i}", [128, c.TC], BF16, off_bytes=2 * c.TC * 2 + i * c.TC * 2) for i in range(2)]
    m_dstg = [at(f"dstg{i}", [128, c.TC], BF16, off_bytes=4 * c.TC * 2 + i * c.TC * 2) for i in range(2)]
    m_yrb = at("yrb", [128, c.D], BF16, off_bytes=0)
    m_outf = at("outf", [128, c.TCSUB * c.D], BF16, off_bytes=8 * KB)
    m_bfs = at("bfs", [128, 8 * KB // 2], BF16, off_bytes=8 * KB)
    m_ostg = at("ostg", [128, c.D], F32, off_bytes=24 * KB)
    m_qT = at("qT", [128, c.NJ * c.SP], BF16, off_bytes=32 * KB)
    m_obf = at("obf", [128, c.NT * c.D], BF16, off_bytes=32 * KB)
    m_hid = at("hid", [128, c.NGH * c.TC], BF16, off_bytes=32 * KB)
    m_xnT = at("xnT", [128, c.NJ * c.GR], BF16, off_bytes=64 * KB)
    m_attn = at("attn", [128, c.NS * c.D], BF16, off_bytes=64 * KB)
    m_y = at("y", [128, c.NT * c.D], BF16, off_bytes=64 * KB)
    m_kt = [at(f"kt{i}", [128, c.SB], BF16, off_bytes=96 * KB + i * c.SB * 2) for i in range(2)]
    m_tT = at("tT", [128, c.NJ * c.SP], BF16, off_bytes=96 * KB)
    RING_N = 3
    m_ring = [at(f"ring{i}", [128, 8 * KB], BF16, off_bytes=128 * KB + i * 16 * KB)
              for i in range(RING_N)]
    m_ring.append(at("ring3", [128, 8 * KB], BF16, off_bytes=112 * KB))
    m_ring.append(at("ring4", [128, 8 * KB], BF16, off_bytes=176 * KB))

    alias_of = {}
    names = list(manual)
    for a in names:
        sa, ea = manual[a]
        alias_of[a] = [b for b in names
                       if b != a and manual[b][0] < ea and sa < manual[b][1]]
    hid_al = list(alias_of["hid"]) + ["hid"]
    for g in range(c.NGH):
        alias_of[f"hid{g}"] = hid_al
    xnT_al = list(alias_of["xnT"]) + ["xnT"]
    alias_of["xnT0"] = xnT_al
    alias_of["xnT1"] = xnT_al
    tT_al = list(alias_of["tT"]) + ["tT"]
    alias_of["tT0"] = tT_al
    alias_of["tT1"] = tT_al
    obf_al = list(alias_of["obf"]) + ["obf"]
    for i in range(c.NT):
        alias_of[f"obf{i}"] = obf_al

    ps = [nc.alloc_psum_tensor(f"ps{i}", [128, 1024], F32) for i in range(4)]

    T = Tracker(alias_of)

    def dma(q, outap, inap, slot, reads, writes, tag=""):
        T.add(q, (lambda e, o=outap, i=inap: e.dma_start(out=o, in_=i)),
              reads=reads, writes=writes, dma_slot=slot, tag=tag)

    ring_i = [0]

    RING_ROT = (0, 1, 2, 4)

    def ring_next():
        i = RING_ROT[ring_i[0] % len(RING_ROT)]
        ring_i[0] += 1
        return i

    def wslot(slot, kj):
        return m_ring[slot][:, kj * 512:(kj + 1) * 512]

    def stream_w_chunk(src, ch, slot, tag, q="pool"):
        src_r = src[:, ch * 512:(ch + 1) * 512].rearrange(
            "(j p) n -> p j n", p=128)
        dma(q, m_ring[slot].ap().rearrange("p (j n) -> p j n", n=512),
            src_r, f"rp{slot}", [], [f"ring{slot}"], tag=tag)

    def ln_stats(src_ap, src_key, ncols, tag, slot=0):
        b = slot * 32
        nch = max(1, ncols // 512)
        w = ncols // nch
        for i in range(nch):
            T.add("dve", (lambda e, i=i, b=b: e.bn_stats(
                sb_st[:, b + i * 6:b + (i + 1) * 6],
                src_ap[:, i * w:(i + 1) * w])),
                reads=[src_key], writes=[f"st6_{slot}"], tag=tag + ".bn")
        T.add("dve", (lambda e, b=b: e.bn_aggr(
            sb_st[:, b + 24:b + 26], sb_st[:, b:b + 6 * nch])),
            reads=[f"st6_{slot}"], writes=[f"stmv_{slot}"], tag=tag + ".ag")
        T.add("dve", (lambda e, b=b: e.tensor_scalar_add(
            sb_st[:, b + 26:b + 27], sb_st[:, b + 25:b + 26], 1e-5)),
            reads=[f"stmv_{slot}"], writes=[f"steps_{slot}"], tag=tag + ".ep")
        T.add("act", (lambda e, b=b: e.activation(
            sb_st[:, b + 27:b + 28], sb_st[:, b + 26:b + 27], AF.Sqrt)),
            reads=[f"steps_{slot}"], writes=[f"stsq_{slot}"], tag=tag + ".sq")
        T.add("dve", (lambda e, b=b: e.reciprocal(
            sb_st[:, b + 28:b + 29], sb_st[:, b + 27:b + 28])),
            reads=[f"stsq_{slot}"], writes=[f"strs_{slot}"], tag=tag + ".rc")

    def ln_stats_act(src_ap, src_key, ncols, tag, slot, scratch_ap, scratch_key):
        b = slot * 32
        T.add("act", (lambda e, b=b: e.activation(
            scratch_ap, src_ap, AF.Identity,
            accum_out=sb_st[:, b + 20:b + 21])),
            reads=[src_key], writes=[scratch_key, f"sacc_{slot}"],
            tag=tag + ".s1")
        T.add("act", (lambda e, b=b: e.activation(
            scratch_ap, src_ap, AF.Square,
            accum_out=sb_st[:, b + 21:b + 22])),
            reads=[src_key], writes=[scratch_key, f"sacc2_{slot}"],
            tag=tag + ".s2")
        inv = 1.0 / ncols
        T.add("dve", (lambda e, b=b: e.tensor_scalar_mul(
            sb_st[:, b + 24:b + 25], sb_st[:, b + 20:b + 21], inv)),
            reads=[f"sacc_{slot}"], writes=[f"stmv_{slot}"], tag=tag + ".m")
        T.add("dve", (lambda e, b=b: e.tensor_scalar_mul(
            sb_st[:, b + 22:b + 23], sb_st[:, b + 21:b + 22], inv)),
            reads=[f"sacc2_{slot}"], writes=[f"st22_{slot}"], tag=tag + ".e2")
        T.add("dve", (lambda e, b=b: e.tensor_mul(
            sb_st[:, b + 23:b + 24], sb_st[:, b + 24:b + 25],
            sb_st[:, b + 24:b + 25])),
            reads=[f"stmv_{slot}"], writes=[f"st23_{slot}"], tag=tag + ".m2")
        T.add("dve", (lambda e, b=b: e.tensor_sub(
            sb_st[:, b + 25:b + 26], sb_st[:, b + 22:b + 23],
            sb_st[:, b + 23:b + 24])),
            reads=[f"st22_{slot}", f"st23_{slot}"], writes=[f"stvar_{slot}"],
            tag=tag + ".v")
        T.add("dve", (lambda e, b=b: e.tensor_scalar_add(
            sb_st[:, b + 26:b + 27], sb_st[:, b + 25:b + 26], 1e-5)),
            reads=[f"stvar_{slot}"], writes=[f"steps_{slot}"], tag=tag + ".ep")
        T.add("act", (lambda e, b=b: e.activation(
            sb_st[:, b + 27:b + 28], sb_st[:, b + 26:b + 27], AF.Sqrt)),
            reads=[f"steps_{slot}"], writes=[f"stsq_{slot}"], tag=tag + ".sq")
        T.add("dve", (lambda e, b=b: e.reciprocal(
            sb_st[:, b + 28:b + 29], sb_st[:, b + 27:b + 28])),
            reads=[f"stsq_{slot}"], writes=[f"strs_{slot}"], tag=tag + ".rc")

    def ln_apply(dst_ap, dst_key, src_ap, src_key, tag, slot=0):
        b = slot * 32
        T.add("dve", (lambda e, b=b: e.tensor_scalar(
            dst_ap, src_ap, sb_st[:, b + 24:b + 25], sb_st[:, b + 28:b + 29],
            ALU.subtract, ALU.mult)),
            reads=[src_key, f"stmv_{slot}", f"strs_{slot}"], writes=[dst_key],
            tag=tag + ".ap")

    def transpose128(dst_ap, dst_key, src_ap, src_key, pi, pb, tag):
        bk = f"p{pi}b{pb}"
        T.add("pe", (lambda e: e.matmul(
            ps[pi][:, pb * 512: pb * 512 + 128], src_ap, sb_ident[:, :],
            start=True, stop=True)),
            reads=[src_key, "ident"], writes=[bk], tag=tag + ".t")
        T.add("dve", (lambda e: e.tensor_copy(
            dst_ap, ps[pi][:, pb * 512: pb * 512 + 128])),
            reads=[bk], writes=[dst_key], tag=tag + ".e")

    def transpose_batch(dst_aps, dst_key, src_aps, src_key, pi, pb, tag, ev_eng="dve"):
        n = len(src_aps)
        bk = f"p{pi}b{pb}"

        def mmt(e, src_aps=src_aps, pi=pi, pb=pb, n=n):
            last = None
            for u, sap in enumerate(src_aps):
                last = e.matmul(
                    ps[pi][:, pb * 512 + u * 128: pb * 512 + (u + 1) * 128],
                    sap, sb_ident[:, :],
                    start=(u == 0), stop=(u == n - 1))
            return last
        T.add("pe", mmt, reads=[src_key, "ident"], writes=[bk], tag=tag + ".t")

        def ev(e, dst_aps=dst_aps, pi=pi, pb=pb, n=n):
            last = None
            for u, dap in enumerate(dst_aps):
                f = e.copy if ev_eng == "act" else e.tensor_copy
                last = f(dap,
                         ps[pi][:, pb * 512 + u * 128: pb * 512 + (u + 1) * 128])
            return last
        T.add(ev_eng, ev, reads=[bk], writes=[dst_key], tag=tag + ".e")

    T.add("dve", lambda e: e.memset(sb_const[:, 0:1], 0.0), writes=["const"])
    T.add("dve", lambda e: e.memset(sb_const[:, 1:2], 1.0), writes=["const"])
    T.add("dve", lambda e: e.memset(sb_const[:, 2:3], c.EXP_SHIFT), writes=["const"])
    T.add("dve", lambda e: e.memset(sb_const[:, 3:4], c.EXP_SCALE), writes=["const"])
    T.add("dve", lambda e: e.memset(sb_ones[:, :], 1.0), writes=["ones"])
    dma("sp", sb_ident[:, :], consts[:, 0:128], "ident", [], ["ident"])
    dma("sp", sb_h128[:, :], consts[:, 128:256], "h128", [], ["h128"])

    for bi in range(4):
        dma("pool", m_btmp[0:1, :], biases[bi:bi + 1, :], "btmp",
            [], ["btmp"], tag=f"b{bi}")
        if bi < 2:
            dstcol = sb_bqcol if bi == 0 else sb_bkcol
            for j in range(c.NJ):
                T.add("pe", (lambda e, j=j: e.matmul(
                    ps[0][:, j:j + 1], m_btmp[0:1, j * 128:(j + 1) * 128],
                    sb_ones[0:1, 0:1], start=(j == 0), stop=(j == c.NJ - 1))),
                    reads=["btmp", "ones"], writes=["p0b0"], tag=f"bc{bi}")
            T.add("dve", (lambda e, dc=dstcol: e.tensor_copy(
                dc[:, :], ps[0][:, 0:c.NJ])),
                reads=["p0b0"], writes=[f"bcol{bi}"], tag=f"bce{bi}")
        else:
            dstb = sb_bvb if bi == 2 else sb_bob
            for ch in range(c.D // 512):
                pb = ch % 2
                T.add("pe", (lambda e, ch=ch, pb=pb: e.matmul(
                    ps[1][:, pb * 512:pb * 512 + 512], sb_ones[0:1, 0:128],
                    m_btmp[0:1, ch * 512:(ch + 1) * 512],
                    start=True, stop=True)),
                    reads=["btmp", "ones"], writes=[f"p1b{pb}"], tag=f"bb{bi}")
                T.add("dve", (lambda e, ch=ch, pb=pb, db=dstb: e.tensor_copy(
                    db[:, ch * 512:(ch + 1) * 512],
                    ps[1][:, pb * 512:pb * 512 + 512])),
                    reads=[f"p1b{pb}"], writes=[f"bbc{bi}"], tag=f"bbe{bi}")

    def emit_group_prep(x_src, rows0, nt, tag):
        for i in range(nt):
            sl = i % 2
            sx = i % 6
            half = f"xnT{(i * 2) // nt}" if nt > 1 else "xnT0"
            dma("pool", m_xg[sl][:, :],
                x_src[rows0 + i * 128: rows0 + (i + 1) * 128, :],
                f"xg{sl}", [], [f"xg{sl}"], tag=f"{tag}.x{i}")
            ln_stats(m_xg[sl], f"xg{sl}", c.D, f"{tag}.l{i}", slot=sl)
            ln_apply(m_xn[sx][:, :], f"xn{sx}", m_xg[sl][:, :], f"xg{sl}",
                     f"{tag}.l{i}", slot=sl)
            for jb in range(c.NJ // 4):
                bk4 = [(2, 0), (2, 1), (3, 0), (3, 1)][jb % 4]
                transpose_batch(
                    [m_xnT[:, (jb * 4 + u) * c.GR + i * 128:
                           (jb * 4 + u) * c.GR + (i + 1) * 128]
                     for u in range(4)],
                    half,
                    [m_xn[sx][:, (jb * 4 + u) * 128:(jb * 4 + u + 1) * 128]
                     for u in range(4)],
                    f"xn{sx}", bk4[0], bk4[1], f"{tag}.tb{i}_{jb}",
                    ev_eng="act")

    RG = [[0, 1, 2, 3], [4, 5, 6, 7]]
    stream_w_chunk(wkT, 0, 0, "k.w0")
    emit_group_prep(x_own, 0, c.NT, "a0")
    for ch in range(c.D // 512):
        slot = ch
        if ch > 0:
            stream_w_chunk(wkT, ch, slot, f"k.w{ch}")
        for jo in range(4):
            jout = ch * 4 + jo
            sl = jout % 2
            for hh in range(c.NHALF):
                bk = (jout * c.NHALF + hh) % 4
                pi, pb = bk // 2, bk % 2

                def mmk(e, slot=slot, jo=jo, hh=hh, pi=pi, pb=pb):
                    last = None
                    for kj in range(c.NJ):
                        last = e.matmul(
                            ps[pi][:, pb * 512: pb * 512 + c.NCH],
                            wslot(slot, kj)[:, jo * 128:(jo + 1) * 128],
                            m_xnT[:, kj * c.SP + hh * c.NCH:
                                  kj * c.SP + (hh + 1) * c.NCH],
                            start=(kj == 0), stop=(kj == c.NJ - 1))
                    return last
                T.add("pe", mmk,
                      reads=[f"ring{slot}",
                             f"xnT{hh if c.NHALF > 1 else 0}"],
                      writes=[f"p{pi}b{pb}"], tag=f"k.m{jout}_{hh}")
                T.add("dve", (lambda e, sl=sl, hh=hh, pi=pi, pb=pb,
                              jout=jout: e.tensor_scalar_add(
                                  m_kts[sl][:, hh * c.NCH:(hh + 1) * c.NCH],
                                  ps[pi][:, pb * 512: pb * 512 + c.NCH],
                                  sb_bkcol[:, jout:jout + 1])),
                      reads=[f"p{pi}b{pb}", "bcol1"], writes=[f"kts{sl}"],
                      tag=f"k.e{jout}_{hh}")
            dma("sp", kv_ok[jout * 128:(jout + 1) * 128, :],
                m_kts[sl][:, :], "kvk", [f"kts{sl}"], ["kvk"],
                tag=f"k.x{jout}")
    for ch in range(c.D // 512):
        slot = ch
        stream_w_chunk(wvT, ch, slot, f"v.w{ch}")
        for i in range(c.NT):
            sl = i % 4
            bk = (ch * c.NT + i) % 4
            pi, pb = bk // 2, bk % 2

            def mmv(e, slot=slot, i=i, pi=pi, pb=pb):
                last = None
                for kj in range(c.NJ):
                    last = e.matmul(
                        ps[pi][:, pb * 512: pb * 512 + 512],
                        m_xnT[:, kj * c.SP + i * 128:
                              kj * c.SP + (i + 1) * 128],
                        wslot(slot, kj),
                        start=(kj == 0), stop=(kj == c.NJ - 1))
                return last
            T.add("pe", mmv,
                  reads=[f"ring{slot}",
                         f"xnT{(i * 2) // c.NT if c.NT > 1 else 0}"],
                  writes=[f"p{pi}b{pb}"], tag=f"v.m{i}_{ch}")
            T.add("dve", (lambda e, sl=sl, ch=ch, pi=pi, pb=pb:
                  e.tensor_tensor(
                      m_vs[sl][:, :],
                      ps[pi][:, pb * 512: pb * 512 + 512],
                      sb_bvb[:, ch * 512:(ch + 1) * 512], ALU.add)),
                  reads=[f"p{pi}b{pb}", "bbc2"], writes=[f"vs{sl}"],
                  tag=f"v.e{i}_{ch}")
            dst = kv_ov.rearrange("(h t) v -> h t v", h=c.NH)[
                4 * ch:4 * ch + 4, i * 128:(i + 1) * 128, :]
            dst2 = dst.transpose([1, 0, 2])
            src2 = m_vs[sl].ap().rearrange("p (h v) -> p h v", v=c.DH)
            dma("sp", dst2, src2, "kvv", [f"vs{sl}"], ["kvv"],
                tag=f"v.x{i}_{ch}")
    T.add("pool", (lambda e: e.collective_compute(
        "AllGather", ALU.bypass, replica_groups=RG,
        ins=[kv_ok.opt()], outs=[kv_agk.opt()])),
        reads=["kvk"], writes=["agk"], dma_slot="ccK", dma_step=None,
        tag="ag.k")
    for ch in range(c.D // 512):
        slot = ch
        stream_w_chunk(wqT, ch, slot, f"q.w{ch}")
        for jo in range(4):
            jout = ch * 4 + jo
            for hh in range(c.NHALF):
                bk = (jout * c.NHALF + hh) % 4
                pi, pb = bk // 2, bk % 2

                def mmq(e, slot=slot, jo=jo, hh=hh, pi=pi, pb=pb):
                    last = None
                    for kj in range(c.NJ):
                        last = e.matmul(
                            ps[pi][:, pb * 512: pb * 512 + c.NCH],
                            wslot(slot, kj)[:, jo * 128:(jo + 1) * 128],
                            m_xnT[:, kj * c.SP + hh * c.NCH:
                                  kj * c.SP + (hh + 1) * c.NCH],
                            start=(kj == 0), stop=(kj == c.NJ - 1))
                    return last
                T.add("pe", mmq,
                      reads=[f"ring{slot}", f"xnT{hh if c.NHALF > 1 else 0}"],
                      writes=[f"p{pi}b{pb}"], tag=f"q.m{jout}_{hh}")
                T.add("dve", (lambda e, jout=jout, hh=hh, pi=pi, pb=pb:
                      e.tensor_scalar_add(
                          m_qT[:, jout * c.SP + hh * c.NCH:
                               jout * c.SP + (hh + 1) * c.NCH],
                          ps[pi][:, pb * 512: pb * 512 + c.NCH],
                          sb_bqcol[:, jout:jout + 1])),
                      reads=[f"p{pi}b{pb}", "bcol0"], writes=["qT"],
                      tag=f"q.e{jout}_{hh}")
    T.add("pool", (lambda e: e.collective_compute(
        "AllGather", ALU.bypass, replica_groups=RG,
        ins=[kv_ov.opt()], outs=[kv_agv.opt()])),
        reads=["kvv"], writes=["agv"], dma_slot="ccV", dma_step=None,
        tag="ag.v")

    def stream_wbf_chunk(src_, ch, slot, tag, dep=None, q="sp"):
        src_r = src_[:, ch * 512:(ch + 1) * 512].rearrange(
            "(j p) n -> p j n", p=128)
        dma(q, m_ring[slot].ap().rearrange("p (j n) -> p j n", n=512),
            src_r, f"rs{slot}", [dep] if dep else [], [f"ring{slot}"],
            tag=tag)


    for s in range(2):
        T.add("dve", (lambda e, s=s: e.memset(
            m_vb[s].ap().rearrange("p (t u) -> p t u", u=c.DH + 1)[:, :, c.DH:c.DH + 1],
            1.0)), writes=[f"vb{s}"], tag="vbone")

    def pvloc(s):
        pi = 2 + s // 4
        pb = (s % 4) // 2
        off = (s % 2) * 129
        return pi, pb, off

    for h in range(c.NH):
        sl = h % 2
        ktsrc = kv_agk.rearrange("(r f) t -> r f t", r=4)[
            :, h * 128:(h + 1) * 128, :].transpose([1, 0, 2])
        dma("sp", m_kt[sl].ap().rearrange("p (r t) -> p r t", r=4), ktsrc,
            f"kt{sl}", ["agk"], [f"kt{sl}"], tag=f"at{h}.kt")
        vsrc = kv_agv.rearrange("(r g i p) v -> r g i p v",
                                r=4, g=c.NH, p=128)[:, h]
        vdst3 = m_vb[sl].ap().rearrange("p (t u) -> p t u", u=c.DH + 1)
        NTR = c.SP // 128
        for r in range(4):
            dma("sp", vdst3[:, r * NTR:(r + 1) * NTR, 0:c.DH],
                vsrc[r].transpose([1, 0, 2]), f"vb{sl}", ["agv"],
                [f"vb{sl}"], tag=f"at{h}.vb{r}")
        def emit_scores(t):
            tp = t % 2
            sc_keys = [f"p{tp}b0"] + ([f"p{tp}b1"] if c.SP > 512 else [])

            def mms(e, h=h, t=t, tp=tp, sl=sl):
                last = None
                for hh in range(c.NHALF):
                    last = e.matmul(
                        ps[tp][:, hh * c.NCH:(hh + 1) * c.NCH],
                        m_kt[sl][:, t * 128:(t + 1) * 128],
                        m_qT[:, h * c.SP + hh * c.NCH:
                             h * c.SP + (hh + 1) * c.NCH],
                        start=True, stop=True)
                return last
            T.add("pe", mms, reads=[f"kt{sl}", "qT"], writes=sc_keys,
                  tag=f"at{h}.s{t}")
            T.add("act", (lambda e, tp=tp: e.activation(
                m_pt[tp][:, :], ps[tp][:, 0:c.SP], AF.Exp,
                bias=c.EXP_SHIFT, scale=c.EXP_SCALE)),
                reads=sc_keys, writes=[f"pt{tp}"], tag=f"at{h}.x{t}")

        emit_scores(0)
        for t in range(c.TAU):
            tp = t % 2
            if t + 1 < c.TAU:
                emit_scores(t + 1)

            def mmpv(e, t=t, tp=tp, sl=sl):
                last = None
                for s in range(c.NS):
                    pi, pb, off = pvloc(s)
                    last = e.matmul(
                        ps[pi][:, pb * 512 + off: pb * 512 + off + c.DH + 1],
                        m_pt[tp][:, s * 128:(s + 1) * 128],
                        m_vb[sl][:, t * (c.DH + 1):(t + 1) * (c.DH + 1)],
                        start=(t == 0 and s % 2 == 0),
                        stop=(t == c.TAU - 1 and (s % 2 == 1 or s == c.NS - 1)))
                return last
            pv_keys = sorted({f"p{pvloc(s)[0]}b{pvloc(s)[1]}"
                              for s in range(c.NS)})
            T.add("pe", mmpv, reads=[f"pt{tp}", f"vb{sl}"], writes=pv_keys,
                  tag=f"at{h}.p{t}")
        for s in range(c.NS):
            pi, pb, off = pvloc(s)
            bk = f"p{pi}b{pb}"
            T.add("dve", (lambda e, pi=pi, pb=pb, off=off: e.reciprocal(
                sb_st[:, 30:31],
                ps[pi][:, pb * 512 + off + c.DH: pb * 512 + off + c.DH + 1])),
                reads=[bk], writes=["strcp"], tag=f"at{h}.r{s}")
            T.add("dve", (lambda e, s=s, h=h, pi=pi, pb=pb, off=off:
                  e.tensor_scalar_mul(
                      m_attn[:, s * c.D + h * 128: s * c.D + (h + 1) * 128],
                      ps[pi][:, pb * 512 + off: pb * 512 + off + c.DH],
                      sb_st[:, 30:31])),
                  reads=[bk, "strcp"], writes=["attn"], tag=f"at{h}.n{s}")

    for s in range(c.NS):
        for jb in range(c.NJ // 4):
            bk4 = [(2, 0), (2, 1), (3, 0), (3, 1)][jb % 4]
            transpose_batch(
                [m_tT[:, (jb * 4 + u) * c.SP + s * 128:
                      (jb * 4 + u) * c.SP + (s + 1) * 128] for u in range(4)],
                f"tT{(s * 2) // c.NS if c.NS > 1 else 0}",
                [m_attn[:, s * c.D + (jb * 4 + u) * 128:
                        s * c.D + (jb * 4 + u + 1) * 128] for u in range(4)],
                "attn", bk4[0], bk4[1], f"c.tb{s}_{jb}", ev_eng="act")
    def c_tail(i):
        sl = i % 2
        dma("pool", m_xrb[sl][:, :], x_own[i * 128:(i + 1) * 128, :],
            f"xrb{sl}", [], [f"xrb{sl}"], tag=f"c.xr{i}")
        ln_stats(m_obf[:, i * c.D:(i + 1) * c.D], f"obf{i}", c.D, f"c.lo{i}",
                 slot=i % 2)
        ln_apply(m_lntmp[:, :], "lntmp",
                 m_obf[:, i * c.D:(i + 1) * c.D], f"obf{i}", f"c.lo{i}",
                 slot=i % 2)
        T.add("dve", (lambda e, i=i, sl=sl: e.tensor_tensor(
            m_y[:, i * c.D:(i + 1) * c.D], m_lntmp[:, :], m_xrb[sl][:, :],
            ALU.add)),
            reads=["lntmp", f"xrb{sl}"], writes=["y"], tag=f"c.y{i}")
        ln_stats(m_y[:, i * c.D:(i + 1) * c.D], "y", c.D, f"c.ly{i}",
                 slot=i % 2)
        ln_apply(m_ystg[:, :], "ystg",
                 m_y[:, i * c.D:(i + 1) * c.D], "y", f"c.ly{i}", slot=i % 2)
        for jb in range(c.NJ // 4):
            bk4 = [(2, 0), (2, 1), (3, 0), (3, 1)][jb % 4]
            transpose_batch(
                [m_tT[:, (jb * 4 + u) * c.SP + i * 128:
                      (jb * 4 + u) * c.SP + (i + 1) * 128] for u in range(4)],
                f"tT{(i * 2) // c.NT if c.NT > 1 else 0}",
                [m_ystg[:, (jb * 4 + u) * 128:(jb * 4 + u + 1) * 128]
                 for u in range(4)], "ystg", bk4[0], bk4[1], f"c.ytb{i}_{jb}",
                ev_eng="act")
        dma("sp", y_dram[i * 128:(i + 1) * 128, :],
            m_y[:, i * c.D:(i + 1) * c.D], "ydr", ["y"], ["ydram"],
            tag=f"c.ys{i}")

    NCHO = c.D // 512
    for ch in range(NCHO):
        slot = ring_next()
        stream_w_chunk(woT, ch, slot, f"o.w{ch}")
        for i in range(c.NS):
            bk = (ch * c.NS + i) % 4
            pi, pb = bk // 2, bk % 2

            def mmo(e, slot=slot, i=i, pi=pi, pb=pb):
                last = None
                for kj in range(c.NJ):
                    last = e.matmul(
                        ps[pi][:, pb * 512: pb * 512 + 512],
                        m_tT[:, kj * c.SP + i * 128: kj * c.SP + (i + 1) * 128],
                        wslot(slot, kj),
                        start=(kj == 0), stop=(kj == c.NJ - 1))
                return last
            T.add("pe", mmo,
                  reads=[f"ring{slot}",
                         f"tT{(i * 2) // c.NS if c.NS > 1 else 0}"],
                  writes=[f"p{pi}b{pb}"], tag=f"o.m{i}_{ch}")
            T.add("dve", (lambda e, i=i, ch=ch, pi=pi, pb=pb:
                  e.tensor_tensor(
                      m_obf[:, i * c.D + ch * 512: i * c.D + (ch + 1) * 512],
                      ps[pi][:, pb * 512: pb * 512 + 512],
                      sb_bob[:, ch * 512:(ch + 1) * 512], ALU.add)),
                  reads=[f"p{pi}b{pb}", "bbc3"], writes=[f"obf{i}"],
                  tag=f"o.e{i}_{ch}")
    for i in range(c.NT):
        c_tail(i)

    for tc in range(c.NTC):
        hid3 = m_hid.ap().rearrange("p (g n) -> p g n", n=c.TC)

        def emit_h128(g, tc=tc):
            gp = g % 2
            T.add("pe", (lambda e, gp=gp: e.matmul(
                ps[2][:, gp * 512: gp * 512 + c.TC], sb_h128[:, :],
                m_hpre[gp][:, :], start=True, stop=True)),
                reads=["h128", f"hpre{gp}"], writes=[f"p2b{gp}"],
                tag=f"f{tc}.h{g}")
            T.add("act", (lambda e, g=g, gp=gp: e.copy(
                m_hid[:, g * c.TC:(g + 1) * c.TC],
                ps[2][:, gp * 512: gp * 512 + c.TC])),
                reads=[f"p2b{gp}"], writes=[f"hid{g}"], tag=f"f{tc}.he{g}")

        bfly_q = []

        def emit_ready_bfly(blk):
            o = 1
            while o < c.NGH:
                for k in range(c.NGH // (2 * o)):
                    if 4 * blk < (k + 1) * 2 * o <= 4 * (blk + 1):
                        bfly_q.append((o, k))
                o *= 2

        def drain_bfly(nmax):
            n = 0
            while bfly_q and n < nmax:
                o, k = bfly_q.pop(0)
                emit_bfly(o, k)
                n += 1

        def emit_bfly(o, k, tc=tc):
            a0 = 2 * o * k
            b0 = a0 + o
            keys = [f"hid{g}" for g in range(a0, a0 + 2 * o)]
            step = max(1, min(o, 16))
            for p0 in range(0, o, step):
                w = min(step, o - p0)
                A = hid3[:, a0 + p0:a0 + p0 + w, :]
                Bp = hid3[:, b0 + p0:b0 + p0 + w, :]
                T.add("dve", (lambda e, A=A, Bp=Bp: e.tensor_add(A, A, Bp)),
                      reads=keys, writes=keys, tag=f"f{tc}.ba{o}")
                T.add("dve", (lambda e, A=A, Bp=Bp: e.scalar_tensor_tensor(
                    Bp, Bp, -2.0, A, ALU.mult, ALU.add)),
                      reads=keys, writes=keys, tag=f"f{tc}.bb{o}")

        for blk in range(c.NGH // 4):
            s1 = ring_next()
            stream_wbf_chunk(w1T, blk, s1, f"f{tc}.w1_{blk}", q="pool")
            s3 = ring_next()
            stream_wbf_chunk(w3T, blk, s3, f"f{tc}.w3_{blk}", q="pool")
            for gg in range(4):
                g = blk * 4 + gg
                gp = g % 2

                def mmu(e, s1=s1, gg=gg, gp=gp, tc=tc):
                    last = None
                    for kj in range(c.NJ):
                        last = e.matmul(
                            ps[0][:, gp * 512: gp * 512 + c.TC],
                            wslot(s1, kj)[:, gg * 128:(gg + 1) * 128],
                            m_tT[:, kj * c.SP + tc * c.TC:
                                 kj * c.SP + (tc + 1) * c.TC],
                            start=(kj == 0), stop=(kj == c.NJ - 1))
                    return last
                T.add("pe", mmu,
                      reads=[f"ring{s1}", f"tT{tc if c.NTC > 1 else 0}"],
                      writes=[f"p0b{gp}"], tag=f"f{tc}.ug{g}")

                def mml(e, s3=s3, gg=gg, gp=gp, tc=tc):
                    last = None
                    for kj in range(c.NJ):
                        last = e.matmul(
                            ps[1][:, gp * 512: gp * 512 + c.TC],
                            wslot(s3, kj)[:, gg * 128:(gg + 1) * 128],
                            m_tT[:, kj * c.SP + tc * c.TC:
                                 kj * c.SP + (tc + 1) * c.TC],
                            start=(kj == 0), stop=(kj == c.NJ - 1))
                    return last
                T.add("pe", mml,
                      reads=[f"ring{s3}", f"tT{tc if c.NTC > 1 else 0}"],
                      writes=[f"p1b{gp}"], tag=f"f{tc}.ul{g}")
                T.add("act", (lambda e, gp=gp: e.activation(
                    m_sil[gp][:, :], ps[0][:, gp * 512: gp * 512 + c.TC],
                    AF.Silu)),
                    reads=[f"p0b{gp}"], writes=[f"sil{gp}"], tag=f"f{tc}.si{g}")
                T.add("dve", (lambda e, gp=gp: e.tensor_tensor(
                    m_hpre[gp][:, :], m_sil[gp][:, :],
                    ps[1][:, gp * 512: gp * 512 + c.TC], ALU.mult)),
                    reads=[f"sil{gp}", f"p1b{gp}"], writes=[f"hpre{gp}"],
                    tag=f"f{tc}.hm{g}")
                if g > 0:
                    emit_h128(g - 1)
                if g == c.NGH - 1:
                    emit_h128(g)
            if blk > 0:
                emit_ready_bfly(blk - 1)
                drain_bfly(10 ** 9)
        emit_ready_bfly(c.NGH // 4 - 1)
        drain_bfly(10 ** 9)
        for dt in range(c.NJ):
            slot = ring_next()
            src_r = w2T[:, dt * 128:(dt + 1) * 128].rearrange(
                "(g p) n -> p g n", p=128)
            dma("pool", m_ring[slot].ap().rearrange(
                "p (g n) -> p g n", n=128)[:, 0:c.NGH, :],
                src_r, f"rs{slot}", [], [f"ring{slot}"],
                tag=f"f{tc}.w2_{dt}")
            pb = dt % 2

            def mmd(e, slot=slot, pb=pb):
                last = None
                for g in range(c.NGH):
                    last = e.matmul(
                        ps[3][:, pb * 512: pb * 512 + c.TC],
                        m_ring[slot][:, g * 128:(g + 1) * 128],
                        m_hid[:, g * c.TC:(g + 1) * c.TC],
                        start=(g == 0), stop=(g == c.NGH - 1))
                return last
            T.add("pe", mmd,
                  reads=[f"ring{slot}"] + [f"hid{g}" for g in range(c.NGH)],
                  writes=[f"p3b{pb}"], tag=f"f{tc}.d{dt}")
            T.add("act", (lambda e, pb=pb: e.copy(
                m_dstg[pb][:, :], ps[3][:, pb * 512: pb * 512 + c.TC])),
                reads=[f"p3b{pb}"], writes=[f"dstg{pb}"], tag=f"f{tc}.de{dt}")
            transpose_batch(
                [m_outf[:, ts * c.D + dt * 128: ts * c.D + (dt + 1) * 128]
                 for ts in range(c.TCSUB)], "outf",
                [m_dstg[pb][:, ts * 128:(ts + 1) * 128]
                 for ts in range(c.TCSUB)], f"dstg{pb}",
                2, dt % 2, f"f{tc}.dtb{dt}")
        for ts in range(c.TCSUB):
            it = tc * c.TCSUB + ts
            dma("sp", m_yrb[:, :], y_dram[it * 128:(it + 1) * 128, :],
                "yrb", ["ydram"], ["yrb"], tag=f"f{tc}.yr{ts}")
            ln_stats(m_outf[:, ts * c.D:(ts + 1) * c.D], "outf", c.D,
                     f"f{tc}.lf{ts}", slot=ts % 2)
            ln_apply(m_ostg[:, :], "ostg",
                     m_outf[:, ts * c.D:(ts + 1) * c.D], "outf",
                     f"f{tc}.lf{ts}", slot=ts % 2)
            T.add("dve", (lambda e: e.tensor_tensor(
                m_ostg[:, :], m_ostg[:, :], m_yrb[:, :], ALU.add)),
                reads=["ostg", "yrb"], writes=["ostg"], tag=f"f{tc}.fa{ts}")
            dma("sp", out[it * 128:(it + 1) * 128, :], m_ostg[:, :],
                "outw", ["ostg"], ["outw"], tag=f"f{tc}.o{ts}")

    sem_names = set()
    for op in T.ops:
        for s, _ in op.waits:
            sem_names.add(s)
        for s, _ in op.incs:
            sem_names.add(s)
    sem_handles = {}

    import contextlib
    stack = contextlib.ExitStack()
    for s in sorted(sem_names):
        sem_handles[s] = stack.enter_context(nc.semaphore(s))

    out_total = T.dma_cum.get("ds_outw", 0)

    with stack:
        with nc.Block() as block:
            def emit_for(eng_name, eng_handle):
                for op in T.ops:
                    if op.eng != eng_name:
                        continue
                    for s, v in op.waits:
                        eng_handle.wait_ge(sem_handles[s], v)
                    inst = op.fn(eng_handle)
                    if inst is not None:
                        for s, v in op.incs:
                            if v is None:
                                inst.then_inc(sem_handles[s])
                            else:
                                inst.then_inc(sem_handles[s], v)
                if eng_name == "sp":
                    eng_handle.wait_ge(sem_handles["ds_outw"], out_total)

            @block.tensor
            def _(e):
                emit_for("pe", e)

            @block.vector
            def _(e):
                emit_for("dve", e)

            @block.scalar
            def _(e):
                emit_for("act", e)

            @block.gpsimd
            def _(e):
                emit_for("pool", e)

            @block.sync
            def _(e):
                emit_for("sp", e)

    return nc



def _hadamard(n):
    M = np.array([[1.0]], dtype=np.float64)
    while M.shape[0] < n:
        M = np.block([[M, M], [M, -M]])
    return M


def make_inputs_for_core(cfg, core, x, wq, bq, wk, bk, wv, bv, wo, bo,
                         W1, W2, W3, shared):
    c = cfg
    xr = x.reshape(-1, c.D)
    return {"x_all": np.ascontiguousarray(xr[core * c.SP:(core + 1) * c.SP]),
            **shared}


def make_shared(cfg, wq, bq, wk, bk, wv, bv, wo, bo, W1, W2, W3):
    c = cfg
    h128 = (_hadamard(128) / np.sqrt(c.HF)).astype(np.float32)
    consts = np.concatenate(
        [np.eye(128, dtype=np.float32), h128], axis=1).astype(bfloat16)
    return {
        "wqT": np.ascontiguousarray(wq.T),
        "wkT": np.ascontiguousarray(wk.T),
        "wvT": np.ascontiguousarray(wv.T),
        "woT": np.ascontiguousarray(wo.T),
        "w1T": np.ascontiguousarray(W1.T),
        "w3T": np.ascontiguousarray(W3.T),
        "w2T": np.ascontiguousarray(W2.T),
        "biases": np.ascontiguousarray(np.stack([bq, bk, bv, bo])),
        "consts": consts,
    }


def kernel(L0_attn_hidden_states, L0_attn_wq, L0_attn_bq, L0_attn_wk,
           L0_attn_bk, L0_attn_wv, L0_attn_bv, L0_attn_wo, L0_attn_bo,
           L0_ffn_W1, L0_ffn_W2, L0_ffn_W3):
    x = np.asarray(L0_attn_hidden_states, dtype=np.float32)
    B, S, D = x.shape
    cfg = Cfg(S_total=B * S, D=D, HF=L0_ffn_W1.shape[0],
              NH=16, n_cores=8)
    nc = build_program(cfg)
    shared = make_shared(cfg, L0_attn_wq, L0_attn_bq, L0_attn_wk, L0_attn_bk,
                         L0_attn_wv, L0_attn_bv, L0_attn_wo, L0_attn_bo,
                         L0_ffn_W1, L0_ffn_W2, L0_ffn_W3)
    shared = {k: np.ascontiguousarray(np.asarray(v, dtype=(
        bfloat16 if k == "consts" else np.float32))) for k, v in shared.items()}
    in_maps = [make_inputs_for_core(cfg, core, x, L0_attn_wq, L0_attn_bq,
                                    L0_attn_wk, L0_attn_bk, L0_attn_wv,
                                    L0_attn_bv, L0_attn_wo, L0_attn_bo,
                                    L0_ffn_W1, L0_ffn_W2, L0_ffn_W3, shared)
               for core in range(cfg.n_cores)]
    res = run_bass_kernel_spmd(nc, in_maps, core_ids=list(range(cfg.n_cores)))
    parts = [np.asarray(res.results[i]["out"]) for i in range(cfg.n_cores)]
    return np.concatenate(parts, axis=0).reshape(B, S, D).astype(np.float32)


# version 30
# speedup vs baseline: 1.0687x; 1.0687x over previous
import numpy as np
import ml_dtypes
import concourse.bass as bass
import concourse.mybir as mybir
from concourse.bass_utils import run_bass_kernel_spmd

F32 = mybir.dt.float32
BF16 = mybir.dt.bfloat16
AF = mybir.ActivationFunctionType
ALU = mybir.AluOpType
bfloat16 = ml_dtypes.bfloat16

ENGINES = ("pe", "act", "dve", "pool", "sp")


class Cfg:
    def __init__(self, S_total=8192, D=2048, HF=8192, NH=16, n_cores=8):
        self.n_cores = n_cores
        self.D = D
        self.HF = HF
        self.NH = NH
        self.DH = D // NH
        self.SP = S_total // n_cores
        self.SB = S_total // 2
        self.NG = 4
        self.GR = self.SB // self.NG
        self.NT = self.SP // 128
        self.NTG = self.GR // 128
        self.NJ = D // 128
        self.NGH = HF // 128
        self.TAU = self.SB // 128
        self.NS = self.SP // 128
        self.NCH = min(512, self.SP)
        self.NHALF = self.SP // self.NCH
        self.GCH = min(512, self.GR)
        self.GHALF = self.GR // self.GCH
        self.TC = min(512, self.SP)
        self.NTC = self.SP // self.TC
        self.TCSUB = self.TC // 128
        self.EXP_SHIFT = -12.0
        self.EXP_SCALE = float(1.0 / np.sqrt(self.DH))



class Op:
    __slots__ = ("eng", "fn", "reads", "writes", "dma_slot", "tag",
                 "waits", "incs")

    def __init__(self, eng, fn, reads, writes, dma_slot, tag):
        self.eng = eng
        self.fn = fn
        self.reads = reads
        self.writes = writes
        self.dma_slot = dma_slot
        self.tag = tag
        self.waits = []
        self.incs = []


class Tracker:
    def __init__(self, alias_of):
        self.alias_of = alias_of
        self.ops = []
        self.seq = {e: 0 for e in ENGINES}
        self.last_w = {}
        self.readers = {}
        self.synced = {e: {} for e in ENGINES}
        self.dma_cum = {}
        self.w_idx = {}

    def chain(self, eng):
        return "ch_" + eng

    def slot_sem(self, slot):
        name = "ds_" + slot
        if name not in self.dma_cum:
            self.dma_cum[name] = 0
        return name

    def _maybe_wait(self, op, eng, dep):
        if dep[0] == "E":
            _, e2, s2 = dep
            if e2 == "pe" and eng == "pe":
                return
            sem = self.chain(e2)
            if self.synced[eng].get(sem, 0) >= s2:
                return
            op.waits.append((sem, s2))
            self.synced[eng][sem] = s2
        else:
            _, sem, cum = dep
            if self.synced[eng].get(sem, 0) >= cum:
                return
            op.waits.append((sem, cum))
            self.synced[eng][sem] = cum

    def add(self, eng, fn, reads=(), writes=(), dma_slot=None, tag="",
            dma_step=16):
        op = Op(eng, fn, list(reads), list(writes), dma_slot, tag)
        deps = {}

        def push(d):
            k = (d[0], d[1])
            if k not in deps or deps[k][2] < d[2]:
                deps[k] = d

        for k in op.reads:
            d = self.last_w.get(k)
            if d is not None:
                push(d)
            own = self.w_idx.get(k, -1)
            for a in self.alias_of.get(k, ()):
                if self.w_idx.get(a, -1) > own:
                    raise AssertionError(
                        f"op {tag!r} reads {k} but alias {a} was written more "
                        f"recently (stale/clobbered data)")
        wkeys = set(op.writes)
        for k in op.writes:
            for a in self.alias_of.get(k, ()):
                wkeys.add(a)
        for k in wkeys:
            d = self.last_w.get(k)
            if d is not None:
                push(d)
            for rk, rv in self.readers.get(k, {}).items():
                push((rk[0], rk[1], rv))
        for d in deps.values():
            self._maybe_wait(op, eng, d)

        if dma_slot is not None:
            sem = self.slot_sem(dma_slot)
            self.dma_cum[sem] += dma_step if dma_step is not None else 1
            op.incs.append((sem, dma_step))
            wtok = ("S", sem, self.dma_cum[sem])
        else:
            self.seq[eng] += 1
            op.incs.append((self.chain(eng), 1))
            wtok = ("E", eng, self.seq[eng])

        for k in op.writes:
            self.last_w[k] = wtok
            self.w_idx[k] = len(self.ops)
            self.readers[k] = {}
        for k in op.reads:
            self.readers.setdefault(k, {})
            self.readers[k][(wtok[0], wtok[1])] = wtok[2]
        self.ops.append(op)
        return op



def build_program(c):
    nc = bass.Bass("TRN2", target_bir_lowering=False)

    x_own = nc.dram_tensor("x_all", [c.SP, c.D], F32, kind="ExternalInput").ap()
    wqT = nc.dram_tensor("wqT", [c.D, c.D], F32, kind="ExternalInput").ap()
    wkT = nc.dram_tensor("wkT", [c.D, c.D], F32, kind="ExternalInput").ap()
    wvT = nc.dram_tensor("wvT", [c.D, c.D], F32, kind="ExternalInput").ap()
    woT = nc.dram_tensor("woT", [c.D, c.D], F32, kind="ExternalInput").ap()
    w1T = nc.dram_tensor("w1T", [c.D, c.HF], F32, kind="ExternalInput").ap()
    w3T = nc.dram_tensor("w3T", [c.D, c.HF], F32, kind="ExternalInput").ap()
    w2T = nc.dram_tensor("w2T", [c.HF, c.D], F32, kind="ExternalInput").ap()
    biases = nc.dram_tensor("biases", [4, c.D], F32, kind="ExternalInput").ap()
    consts = nc.dram_tensor("consts", [128, 256], BF16, kind="ExternalInput").ap()
    out = nc.dram_tensor("out", [c.SP, c.D], F32, kind="ExternalOutput").ap()

    kv_ok = nc.dram_tensor("kv_ok", [c.D, c.SP], BF16).ap()
    kv_ov = nc.dram_tensor("kv_ov", [c.NH * c.SP, c.DH], BF16).ap()
    kv_agk = nc.dram_tensor("kv_agk", [4 * c.D, c.SP], BF16).ap()
    kv_agv = nc.dram_tensor("kv_agv", [4 * c.NH * c.SP, c.DH], BF16).ap()
    y_dram = nc.dram_tensor("y_dram", [c.SP, c.D], BF16).ap()

    sb_const = nc.alloc_sbuf_tensor("sb_const", [128, 8], F32)
    nc.const_aps.register(F32, 0.0, sb_const[:, 0:1])
    nc.const_aps.register(F32, 1.0, sb_const[:, 1:2])
    nc.const_aps.register(F32, c.EXP_SHIFT, sb_const[:, 2:3])
    nc.const_aps.register(F32, c.EXP_SCALE, sb_const[:, 3:4])
    sb_ones = nc.alloc_sbuf_tensor("sb_ones", [1, 128], BF16)
    sb_ident = nc.alloc_sbuf_tensor("sb_ident", [128, 128], BF16)
    sb_h128 = nc.alloc_sbuf_tensor("sb_h128", [128, 128], BF16)
    sb_bqcol = nc.alloc_sbuf_tensor("sb_bqcol", [128, c.NJ], F32)
    sb_bkcol = nc.alloc_sbuf_tensor("sb_bkcol", [128, c.NJ], F32)
    sb_bvb = nc.alloc_sbuf_tensor("sb_bvb", [128, c.D], BF16)
    sb_bob = nc.alloc_sbuf_tensor("sb_bob", [128, c.D], BF16)
    sb_st = nc.alloc_sbuf_tensor("sb_st", [128, 80], F32)

    KB = 1024
    BASE = 28 * KB
    manual = {}

    def at(name, shape, dtype, off_kb_x2=None, off_bytes=None):
        off = off_bytes if off_bytes is not None else 0
        off = (off + 31) // 32 * 32
        h = nc.alloc_sbuf_tensor_at(name, shape, dtype, offset=BASE + off)
        nbytes = int(np.prod(shape[1:])) * (2 if dtype == BF16 else 4)
        manual[name] = (off, off + nbytes)
        return h

    SPB = c.SP * 2
    m_xg = [at(f"xg{i}", [128, c.D], BF16, off_bytes=i * 4 * KB) for i in range(2)]
    m_xn = [at(f"xn{i}", [128, c.D], BF16, off_bytes=8 * KB + i * 4 * KB) for i in range(6)]
    m_kts = [at(f"kts{i}", [128, c.GR], BF16, off_bytes=96 * KB + i * 2 * c.GR) for i in range(2)]
    m_vs = [at(f"vs{i}", [128, 512], BF16, off_bytes=100 * KB + i * KB) for i in range(4)]
    m_btmp = at("btmp", [1, c.D], BF16, off_bytes=104 * KB)
    VBB = (c.TAU * (c.DH + 1) * 2 + 31) // 32 * 32
    m_vb = [at(f"vb{i}", [128, c.TAU * (c.DH + 1)], BF16, off_bytes=i * VBB) for i in range(2)]
    m_pt = [at(f"pt{i}", [128, c.SP], BF16, off_bytes=2 * VBB + i * SPB) for i in range(2)]
    m_lntmp = at("lntmp", [128, c.D], F32, off_bytes=0)
    m_xrb = [at(f"xrb{i}", [128, c.D], BF16, off_bytes=8 * KB + i * 4 * KB) for i in range(2)]
    m_ystg = at("ystg", [128, c.D], BF16, off_bytes=16 * KB)
    m_sil = [at(f"sil{i}", [128, c.TC], BF16, off_bytes=i * c.TC * 2) for i in range(2)]
    m_hpre = [at(f"hpre{i}", [128, c.TC], BF16, off_bytes=2 * c.TC * 2 + i * c.TC * 2) for i in range(2)]
    m_dstg = [at(f"dstg{i}", [128, c.TC], BF16, off_bytes=4 * c.TC * 2 + i * c.TC * 2) for i in range(2)]
    m_yrb = at("yrb", [128, c.D], BF16, off_bytes=0)
    m_outf = at("outf", [128, c.TCSUB * c.D], BF16, off_bytes=8 * KB)
    m_bfs = at("bfs", [128, 8 * KB // 2], BF16, off_bytes=8 * KB)
    m_ostg = at("ostg", [128, c.D], F32, off_bytes=24 * KB)
    m_qT = at("qT", [128, c.NJ * c.SP], BF16, off_bytes=32 * KB)
    m_obf = at("obf", [128, c.NT * c.D], BF16, off_bytes=32 * KB)
    m_hid = at("hid", [128, c.NGH * c.TC], BF16, off_bytes=32 * KB)
    m_xnT = at("xnT", [128, c.NJ * c.GR], BF16, off_bytes=64 * KB)
    m_attn = at("attn", [128, c.NS * c.D], BF16, off_bytes=64 * KB)
    m_y = at("y", [128, c.NT * c.D], BF16, off_bytes=64 * KB)
    m_kt = [at(f"kt{i}", [128, c.SB], BF16, off_bytes=96 * KB + i * c.SB * 2) for i in range(2)]
    m_tT = at("tT", [128, c.NJ * c.SP], BF16, off_bytes=96 * KB)
    RING_N = 3
    m_ring = [at(f"ring{i}", [128, 8 * KB], BF16, off_bytes=128 * KB + i * 16 * KB)
              for i in range(RING_N)]
    m_ring.append(at("ring3", [128, 8 * KB], BF16, off_bytes=112 * KB))
    m_ring.append(at("ring4", [128, 8 * KB], BF16, off_bytes=176 * KB))

    alias_of = {}
    names = list(manual)
    for a in names:
        sa, ea = manual[a]
        alias_of[a] = [b for b in names
                       if b != a and manual[b][0] < ea and sa < manual[b][1]]
    hid_al = list(alias_of["hid"]) + ["hid"]
    for g in range(c.NGH):
        alias_of[f"hid{g}"] = hid_al
    xnT_al = list(alias_of["xnT"]) + ["xnT"]
    alias_of["xnT0"] = xnT_al
    alias_of["xnT1"] = xnT_al
    tT_al = list(alias_of["tT"]) + ["tT"]
    alias_of["tT0"] = tT_al
    alias_of["tT1"] = tT_al
    obf_al = list(alias_of["obf"]) + ["obf"]
    for i in range(c.NT):
        alias_of[f"obf{i}"] = obf_al

    ps = [nc.alloc_psum_tensor(f"ps{i}", [128, 1024], F32) for i in range(4)]

    T = Tracker(alias_of)

    def dma(q, outap, inap, slot, reads, writes, tag=""):
        T.add(q, (lambda e, o=outap, i=inap: e.dma_start(out=o, in_=i)),
              reads=reads, writes=writes, dma_slot=slot, tag=tag)

    ring_i = [0]

    RING_ROT = (0, 1, 2, 4)

    def ring_next():
        i = RING_ROT[ring_i[0] % len(RING_ROT)]
        ring_i[0] += 1
        return i

    def wslot(slot, kj):
        return m_ring[slot][:, kj * 512:(kj + 1) * 512]

    def stream_w_chunk(src, ch, slot, tag, q="pool"):
        src_r = src[:, ch * 512:(ch + 1) * 512].rearrange(
            "(j p) n -> p j n", p=128)
        dma(q, m_ring[slot].ap().rearrange("p (j n) -> p j n", n=512),
            src_r, f"rp{slot}", [], [f"ring{slot}"], tag=tag)

    def ln_stats(src_ap, src_key, ncols, tag, slot=0):
        b = slot * 32
        nch = max(1, ncols // 512)
        w = ncols // nch
        for i in range(nch):
            T.add("dve", (lambda e, i=i, b=b: e.bn_stats(
                sb_st[:, b + i * 6:b + (i + 1) * 6],
                src_ap[:, i * w:(i + 1) * w])),
                reads=[src_key], writes=[f"st6_{slot}"], tag=tag + ".bn")
        T.add("dve", (lambda e, b=b: e.bn_aggr(
            sb_st[:, b + 24:b + 26], sb_st[:, b:b + 6 * nch])),
            reads=[f"st6_{slot}"], writes=[f"stmv_{slot}"], tag=tag + ".ag")
        T.add("dve", (lambda e, b=b: e.tensor_scalar_add(
            sb_st[:, b + 26:b + 27], sb_st[:, b + 25:b + 26], 1e-5)),
            reads=[f"stmv_{slot}"], writes=[f"steps_{slot}"], tag=tag + ".ep")
        T.add("act", (lambda e, b=b: e.activation(
            sb_st[:, b + 27:b + 28], sb_st[:, b + 26:b + 27], AF.Sqrt)),
            reads=[f"steps_{slot}"], writes=[f"stsq_{slot}"], tag=tag + ".sq")
        T.add("dve", (lambda e, b=b: e.reciprocal(
            sb_st[:, b + 28:b + 29], sb_st[:, b + 27:b + 28])),
            reads=[f"stsq_{slot}"], writes=[f"strs_{slot}"], tag=tag + ".rc")

    def ln_stats_act(src_ap, src_key, ncols, tag, slot, scratch_ap, scratch_key):
        b = slot * 32
        T.add("act", (lambda e, b=b: e.activation(
            scratch_ap, src_ap, AF.Identity,
            accum_out=sb_st[:, b + 20:b + 21])),
            reads=[src_key], writes=[scratch_key, f"sacc_{slot}"],
            tag=tag + ".s1")
        T.add("act", (lambda e, b=b: e.activation(
            scratch_ap, src_ap, AF.Square,
            accum_out=sb_st[:, b + 21:b + 22])),
            reads=[src_key], writes=[scratch_key, f"sacc2_{slot}"],
            tag=tag + ".s2")
        inv = 1.0 / ncols
        T.add("dve", (lambda e, b=b: e.tensor_scalar_mul(
            sb_st[:, b + 24:b + 25], sb_st[:, b + 20:b + 21], inv)),
            reads=[f"sacc_{slot}"], writes=[f"stmv_{slot}"], tag=tag + ".m")
        T.add("dve", (lambda e, b=b: e.tensor_scalar_mul(
            sb_st[:, b + 22:b + 23], sb_st[:, b + 21:b + 22], inv)),
            reads=[f"sacc2_{slot}"], writes=[f"st22_{slot}"], tag=tag + ".e2")
        T.add("dve", (lambda e, b=b: e.tensor_mul(
            sb_st[:, b + 23:b + 24], sb_st[:, b + 24:b + 25],
            sb_st[:, b + 24:b + 25])),
            reads=[f"stmv_{slot}"], writes=[f"st23_{slot}"], tag=tag + ".m2")
        T.add("dve", (lambda e, b=b: e.tensor_sub(
            sb_st[:, b + 25:b + 26], sb_st[:, b + 22:b + 23],
            sb_st[:, b + 23:b + 24])),
            reads=[f"st22_{slot}", f"st23_{slot}"], writes=[f"stvar_{slot}"],
            tag=tag + ".v")
        T.add("dve", (lambda e, b=b: e.tensor_scalar_add(
            sb_st[:, b + 26:b + 27], sb_st[:, b + 25:b + 26], 1e-5)),
            reads=[f"stvar_{slot}"], writes=[f"steps_{slot}"], tag=tag + ".ep")
        T.add("act", (lambda e, b=b: e.activation(
            sb_st[:, b + 27:b + 28], sb_st[:, b + 26:b + 27], AF.Sqrt)),
            reads=[f"steps_{slot}"], writes=[f"stsq_{slot}"], tag=tag + ".sq")
        T.add("dve", (lambda e, b=b: e.reciprocal(
            sb_st[:, b + 28:b + 29], sb_st[:, b + 27:b + 28])),
            reads=[f"stsq_{slot}"], writes=[f"strs_{slot}"], tag=tag + ".rc")

    def ln_apply(dst_ap, dst_key, src_ap, src_key, tag, slot=0):
        b = slot * 32
        T.add("dve", (lambda e, b=b: e.tensor_scalar(
            dst_ap, src_ap, sb_st[:, b + 24:b + 25], sb_st[:, b + 28:b + 29],
            ALU.subtract, ALU.mult)),
            reads=[src_key, f"stmv_{slot}", f"strs_{slot}"], writes=[dst_key],
            tag=tag + ".ap")

    def transpose128(dst_ap, dst_key, src_ap, src_key, pi, pb, tag):
        bk = f"p{pi}b{pb}"
        T.add("pe", (lambda e: e.matmul(
            ps[pi][:, pb * 512: pb * 512 + 128], src_ap, sb_ident[:, :],
            start=True, stop=True)),
            reads=[src_key, "ident"], writes=[bk], tag=tag + ".t")
        T.add("dve", (lambda e: e.tensor_copy(
            dst_ap, ps[pi][:, pb * 512: pb * 512 + 128])),
            reads=[bk], writes=[dst_key], tag=tag + ".e")

    def transpose_batch(dst_aps, dst_key, src_aps, src_key, pi, pb, tag, ev_eng="dve"):
        n = len(src_aps)
        bk = f"p{pi}b{pb}"

        def mmt(e, src_aps=src_aps, pi=pi, pb=pb, n=n):
            last = None
            for u, sap in enumerate(src_aps):
                last = e.matmul(
                    ps[pi][:, pb * 512 + u * 128: pb * 512 + (u + 1) * 128],
                    sap, sb_ident[:, :],
                    start=(u == 0), stop=(u == n - 1))
            return last
        T.add("pe", mmt, reads=[src_key, "ident"], writes=[bk], tag=tag + ".t")

        def ev(e, dst_aps=dst_aps, pi=pi, pb=pb, n=n):
            last = None
            for u, dap in enumerate(dst_aps):
                f = e.copy if ev_eng == "act" else e.tensor_copy
                last = f(dap,
                         ps[pi][:, pb * 512 + u * 128: pb * 512 + (u + 1) * 128])
            return last
        T.add(ev_eng, ev, reads=[bk], writes=[dst_key], tag=tag + ".e")

    T.add("dve", lambda e: e.memset(sb_const[:, 0:1], 0.0), writes=["const"])
    T.add("dve", lambda e: e.memset(sb_const[:, 1:2], 1.0), writes=["const"])
    T.add("dve", lambda e: e.memset(sb_const[:, 2:3], c.EXP_SHIFT), writes=["const"])
    T.add("dve", lambda e: e.memset(sb_const[:, 3:4], c.EXP_SCALE), writes=["const"])
    T.add("dve", lambda e: e.memset(sb_ones[:, :], 1.0), writes=["ones"])
    dma("sp", sb_ident[:, :], consts[:, 0:128], "ident", [], ["ident"])
    dma("sp", sb_h128[:, :], consts[:, 128:256], "h128", [], ["h128"])

    for bi in range(4):
        dma("pool", m_btmp[0:1, :], biases[bi:bi + 1, :], "btmp",
            [], ["btmp"], tag=f"b{bi}")
        if bi < 2:
            dstcol = sb_bqcol if bi == 0 else sb_bkcol
            for j in range(c.NJ):
                T.add("pe", (lambda e, j=j: e.matmul(
                    ps[0][:, j:j + 1], m_btmp[0:1, j * 128:(j + 1) * 128],
                    sb_ones[0:1, 0:1], start=(j == 0), stop=(j == c.NJ - 1))),
                    reads=["btmp", "ones"], writes=["p0b0"], tag=f"bc{bi}")
            T.add("dve", (lambda e, dc=dstcol: e.tensor_copy(
                dc[:, :], ps[0][:, 0:c.NJ])),
                reads=["p0b0"], writes=[f"bcol{bi}"], tag=f"bce{bi}")
        else:
            dstb = sb_bvb if bi == 2 else sb_bob
            for ch in range(c.D // 512):
                pb = ch % 2
                T.add("pe", (lambda e, ch=ch, pb=pb: e.matmul(
                    ps[1][:, pb * 512:pb * 512 + 512], sb_ones[0:1, 0:128],
                    m_btmp[0:1, ch * 512:(ch + 1) * 512],
                    start=True, stop=True)),
                    reads=["btmp", "ones"], writes=[f"p1b{pb}"], tag=f"bb{bi}")
                T.add("dve", (lambda e, ch=ch, pb=pb, db=dstb: e.tensor_copy(
                    db[:, ch * 512:(ch + 1) * 512],
                    ps[1][:, pb * 512:pb * 512 + 512])),
                    reads=[f"p1b{pb}"], writes=[f"bbc{bi}"], tag=f"bbe{bi}")

    def emit_group_prep(x_src, rows0, nt, tag):
        for i in range(nt):
            sl = i % 2
            sx = i % 6
            half = f"xnT{(i * 2) // nt}" if nt > 1 else "xnT0"
            dma("pool", m_xg[sl][:, :],
                x_src[rows0 + i * 128: rows0 + (i + 1) * 128, :],
                f"xg{sl}", [], [f"xg{sl}"], tag=f"{tag}.x{i}")
            ln_stats(m_xg[sl], f"xg{sl}", c.D, f"{tag}.l{i}", slot=sl)
            ln_apply(m_xn[sx][:, :], f"xn{sx}", m_xg[sl][:, :], f"xg{sl}",
                     f"{tag}.l{i}", slot=sl)
            for jb in range(c.NJ // 4):
                bk4 = [(2, 0), (2, 1), (3, 0), (3, 1)][jb % 4]
                transpose_batch(
                    [m_xnT[:, (jb * 4 + u) * c.GR + i * 128:
                           (jb * 4 + u) * c.GR + (i + 1) * 128]
                     for u in range(4)],
                    half,
                    [m_xn[sx][:, (jb * 4 + u) * 128:(jb * 4 + u + 1) * 128]
                     for u in range(4)],
                    f"xn{sx}", bk4[0], bk4[1], f"{tag}.tb{i}_{jb}",
                    ev_eng="act")

    RG = [[0, 1, 2, 3], [4, 5, 6, 7]]
    stream_w_chunk(wkT, 0, 0, "k.w0")
    emit_group_prep(x_own, 0, c.NT, "a0")
    for ch in range(c.D // 512):
        slot = ch
        if ch > 0:
            stream_w_chunk(wkT, ch, slot, f"k.w{ch}")
        for jo in range(4):
            jout = ch * 4 + jo
            sl = jout % 2
            for hh in range(c.NHALF):
                bk = (jout * c.NHALF + hh) % 4
                pi, pb = bk // 2, bk % 2

                def mmk(e, slot=slot, jo=jo, hh=hh, pi=pi, pb=pb):
                    last = None
                    for kj in range(c.NJ):
                        last = e.matmul(
                            ps[pi][:, pb * 512: pb * 512 + c.NCH],
                            wslot(slot, kj)[:, jo * 128:(jo + 1) * 128],
                            m_xnT[:, kj * c.SP + hh * c.NCH:
                                  kj * c.SP + (hh + 1) * c.NCH],
                            start=(kj == 0), stop=(kj == c.NJ - 1))
                    return last
                T.add("pe", mmk,
                      reads=[f"ring{slot}",
                             f"xnT{hh if c.NHALF > 1 else 0}"],
                      writes=[f"p{pi}b{pb}"], tag=f"k.m{jout}_{hh}")
                T.add("dve", (lambda e, sl=sl, hh=hh, pi=pi, pb=pb,
                              jout=jout: e.tensor_scalar_add(
                                  m_kts[sl][:, hh * c.NCH:(hh + 1) * c.NCH],
                                  ps[pi][:, pb * 512: pb * 512 + c.NCH],
                                  sb_bkcol[:, jout:jout + 1])),
                      reads=[f"p{pi}b{pb}", "bcol1"], writes=[f"kts{sl}"],
                      tag=f"k.e{jout}_{hh}")
            dma("sp", kv_ok[jout * 128:(jout + 1) * 128, :],
                m_kts[sl][:, :], "kvk", [f"kts{sl}"], ["kvk"],
                tag=f"k.x{jout}")
    for ch in range(c.D // 512):
        slot = ch
        stream_w_chunk(wvT, ch, slot, f"v.w{ch}")
        for i in range(c.NT):
            sl = i % 4
            bk = (ch * c.NT + i) % 4
            pi, pb = bk // 2, bk % 2

            def mmv(e, slot=slot, i=i, pi=pi, pb=pb):
                last = None
                for kj in range(c.NJ):
                    last = e.matmul(
                        ps[pi][:, pb * 512: pb * 512 + 512],
                        m_xnT[:, kj * c.SP + i * 128:
                              kj * c.SP + (i + 1) * 128],
                        wslot(slot, kj),
                        start=(kj == 0), stop=(kj == c.NJ - 1))
                return last
            T.add("pe", mmv,
                  reads=[f"ring{slot}",
                         f"xnT{(i * 2) // c.NT if c.NT > 1 else 0}"],
                  writes=[f"p{pi}b{pb}"], tag=f"v.m{i}_{ch}")
            T.add("dve", (lambda e, sl=sl, ch=ch, pi=pi, pb=pb:
                  e.tensor_tensor(
                      m_vs[sl][:, :],
                      ps[pi][:, pb * 512: pb * 512 + 512],
                      sb_bvb[:, ch * 512:(ch + 1) * 512], ALU.add)),
                  reads=[f"p{pi}b{pb}", "bbc2"], writes=[f"vs{sl}"],
                  tag=f"v.e{i}_{ch}")
            dst = kv_ov.rearrange("(h t) v -> h t v", h=c.NH)[
                4 * ch:4 * ch + 4, i * 128:(i + 1) * 128, :]
            dst2 = dst.transpose([1, 0, 2])
            src2 = m_vs[sl].ap().rearrange("p (h v) -> p h v", v=c.DH)
            dma("sp", dst2, src2, "kvv", [f"vs{sl}"], ["kvv"],
                tag=f"v.x{i}_{ch}")
    T.add("pool", (lambda e: e.collective_compute(
        "AllGather", ALU.bypass, replica_groups=RG,
        ins=[kv_ok.opt()], outs=[kv_agk.opt()])),
        reads=["kvk"], writes=["agk"], dma_slot="ccK", dma_step=None,
        tag="ag.k")
    for ch in range(c.D // 512):
        slot = ch
        stream_w_chunk(wqT, ch, slot, f"q.w{ch}")
        for jo in range(4):
            jout = ch * 4 + jo
            for hh in range(c.NHALF):
                bk = (jout * c.NHALF + hh) % 4
                pi, pb = bk // 2, bk % 2

                def mmq(e, slot=slot, jo=jo, hh=hh, pi=pi, pb=pb):
                    last = None
                    for kj in range(c.NJ):
                        last = e.matmul(
                            ps[pi][:, pb * 512: pb * 512 + c.NCH],
                            wslot(slot, kj)[:, jo * 128:(jo + 1) * 128],
                            m_xnT[:, kj * c.SP + hh * c.NCH:
                                  kj * c.SP + (hh + 1) * c.NCH],
                            start=(kj == 0), stop=(kj == c.NJ - 1))
                    return last
                T.add("pe", mmq,
                      reads=[f"ring{slot}", f"xnT{hh if c.NHALF > 1 else 0}"],
                      writes=[f"p{pi}b{pb}"], tag=f"q.m{jout}_{hh}")
                T.add("dve", (lambda e, jout=jout, hh=hh, pi=pi, pb=pb:
                      e.tensor_scalar_add(
                          m_qT[:, jout * c.SP + hh * c.NCH:
                               jout * c.SP + (hh + 1) * c.NCH],
                          ps[pi][:, pb * 512: pb * 512 + c.NCH],
                          sb_bqcol[:, jout:jout + 1])),
                      reads=[f"p{pi}b{pb}", "bcol0"], writes=["qT"],
                      tag=f"q.e{jout}_{hh}")
    T.add("pool", (lambda e: e.collective_compute(
        "AllGather", ALU.bypass, replica_groups=RG,
        ins=[kv_ov.opt()], outs=[kv_agv.opt()])),
        reads=["kvv"], writes=["agv"], dma_slot="ccV", dma_step=None,
        tag="ag.v")

    def stream_wbf_chunk(src_, ch, slot, tag, dep=None, q="sp"):
        src_r = src_[:, ch * 512:(ch + 1) * 512].rearrange(
            "(j p) n -> p j n", p=128)
        dma(q, m_ring[slot].ap().rearrange("p (j n) -> p j n", n=512),
            src_r, f"rs{slot}", [dep] if dep else [], [f"ring{slot}"],
            tag=tag)


    for s in range(2):
        T.add("dve", (lambda e, s=s: e.memset(
            m_vb[s].ap().rearrange("p (t u) -> p t u", u=c.DH + 1)[:, :, c.DH:c.DH + 1],
            1.0)), writes=[f"vb{s}"], tag="vbone")

    def pvloc(s):
        pi = 2 + s // 4
        pb = (s % 4) // 2
        off = (s % 2) * 129
        return pi, pb, off

    for h in range(c.NH):
        sl = h % 2
        ktsrc = kv_agk.rearrange("(r f) t -> r f t", r=4)[
            :, h * 128:(h + 1) * 128, :].transpose([1, 0, 2])
        dma("sp", m_kt[sl].ap().rearrange("p (r t) -> p r t", r=4), ktsrc,
            f"kt{sl}", ["agk"], [f"kt{sl}"], tag=f"at{h}.kt")
        vsrc = kv_agv.rearrange("(r g i p) v -> r g i p v",
                                r=4, g=c.NH, p=128)[:, h]
        vdst3 = m_vb[sl].ap().rearrange("p (t u) -> p t u", u=c.DH + 1)
        NTR = c.SP // 128
        for r in range(4):
            dma("sp", vdst3[:, r * NTR:(r + 1) * NTR, 0:c.DH],
                vsrc[r].transpose([1, 0, 2]), f"vb{sl}", ["agv"],
                [f"vb{sl}"], tag=f"at{h}.vb{r}")
        def emit_scores(t):
            tp = t % 2
            sc_keys = [f"p{tp}b0"] + ([f"p{tp}b1"] if c.SP > 512 else [])

            def mms(e, h=h, t=t, tp=tp, sl=sl):
                last = None
                for hh in range(c.NHALF):
                    last = e.matmul(
                        ps[tp][:, hh * c.NCH:(hh + 1) * c.NCH],
                        m_kt[sl][:, t * 128:(t + 1) * 128],
                        m_qT[:, h * c.SP + hh * c.NCH:
                             h * c.SP + (hh + 1) * c.NCH],
                        start=True, stop=True)
                return last
            T.add("pe", mms, reads=[f"kt{sl}", "qT"], writes=sc_keys,
                  tag=f"at{h}.s{t}")
            T.add("act", (lambda e, tp=tp: e.activation(
                m_pt[tp][:, :], ps[tp][:, 0:c.SP], AF.Exp,
                bias=c.EXP_SHIFT, scale=c.EXP_SCALE)),
                reads=sc_keys, writes=[f"pt{tp}"], tag=f"at{h}.x{t}")

        emit_scores(0)
        for t in range(c.TAU):
            tp = t % 2
            if t + 1 < c.TAU:
                emit_scores(t + 1)

            def mmpv(e, t=t, tp=tp, sl=sl):
                last = None
                for s in range(c.NS):
                    pi, pb, off = pvloc(s)
                    last = e.matmul(
                        ps[pi][:, pb * 512 + off: pb * 512 + off + c.DH + 1],
                        m_pt[tp][:, s * 128:(s + 1) * 128],
                        m_vb[sl][:, t * (c.DH + 1):(t + 1) * (c.DH + 1)],
                        start=(t == 0 and s % 2 == 0),
                        stop=(t == c.TAU - 1 and (s % 2 == 1 or s == c.NS - 1)))
                return last
            pv_keys = sorted({f"p{pvloc(s)[0]}b{pvloc(s)[1]}"
                              for s in range(c.NS)})
            T.add("pe", mmpv, reads=[f"pt{tp}", f"vb{sl}"], writes=pv_keys,
                  tag=f"at{h}.p{t}")
        for s in range(c.NS):
            pi, pb, off = pvloc(s)
            bk = f"p{pi}b{pb}"
            T.add("dve", (lambda e, pi=pi, pb=pb, off=off: e.reciprocal(
                sb_st[:, 30:31],
                ps[pi][:, pb * 512 + off + c.DH: pb * 512 + off + c.DH + 1])),
                reads=[bk], writes=["strcp"], tag=f"at{h}.r{s}")
            T.add("dve", (lambda e, s=s, h=h, pi=pi, pb=pb, off=off:
                  e.tensor_scalar_mul(
                      m_attn[:, s * c.D + h * 128: s * c.D + (h + 1) * 128],
                      ps[pi][:, pb * 512 + off: pb * 512 + off + c.DH],
                      sb_st[:, 30:31])),
                  reads=[bk, "strcp"], writes=["attn"], tag=f"at{h}.n{s}")

    for s in range(c.NS):
        for jb in range(c.NJ // 4):
            bk4 = [(2, 0), (2, 1), (3, 0), (3, 1)][jb % 4]
            transpose_batch(
                [m_tT[:, (jb * 4 + u) * c.SP + s * 128:
                      (jb * 4 + u) * c.SP + (s + 1) * 128] for u in range(4)],
                f"tT{(s * 2) // c.NS if c.NS > 1 else 0}",
                [m_attn[:, s * c.D + (jb * 4 + u) * 128:
                        s * c.D + (jb * 4 + u + 1) * 128] for u in range(4)],
                "attn", bk4[0], bk4[1], f"c.tb{s}_{jb}", ev_eng="act")
    def c_tail(i):
        sl = i % 2
        dma("pool", m_xrb[sl][:, :], x_own[i * 128:(i + 1) * 128, :],
            f"xrb{sl}", [], [f"xrb{sl}"], tag=f"c.xr{i}")
        ln_stats(m_obf[:, i * c.D:(i + 1) * c.D], f"obf{i}", c.D, f"c.lo{i}",
                 slot=i % 2)
        ln_apply(m_lntmp[:, :], "lntmp",
                 m_obf[:, i * c.D:(i + 1) * c.D], f"obf{i}", f"c.lo{i}",
                 slot=i % 2)
        T.add("pool", (lambda e, i=i, sl=sl: e.tensor_tensor(
            m_y[:, i * c.D:(i + 1) * c.D], m_lntmp[:, :], m_xrb[sl][:, :],
            ALU.add)),
            reads=["lntmp", f"xrb{sl}"], writes=["y"], tag=f"c.y{i}")
        ln_stats(m_y[:, i * c.D:(i + 1) * c.D], "y", c.D, f"c.ly{i}",
                 slot=i % 2)
        ln_apply(m_ystg[:, :], "ystg",
                 m_y[:, i * c.D:(i + 1) * c.D], "y", f"c.ly{i}", slot=i % 2)
        for jb in range(c.NJ // 4):
            bk4 = [(2, 0), (2, 1), (3, 0), (3, 1)][jb % 4]
            transpose_batch(
                [m_tT[:, (jb * 4 + u) * c.SP + i * 128:
                      (jb * 4 + u) * c.SP + (i + 1) * 128] for u in range(4)],
                f"tT{(i * 2) // c.NT if c.NT > 1 else 0}",
                [m_ystg[:, (jb * 4 + u) * 128:(jb * 4 + u + 1) * 128]
                 for u in range(4)], "ystg", bk4[0], bk4[1], f"c.ytb{i}_{jb}",
                ev_eng="act")
        dma("sp", y_dram[i * 128:(i + 1) * 128, :],
            m_y[:, i * c.D:(i + 1) * c.D], "ydr", ["y"], ["ydram"],
            tag=f"c.ys{i}")

    NCHO = c.D // 512
    for ch in range(NCHO):
        slot = ring_next()
        stream_w_chunk(woT, ch, slot, f"o.w{ch}")
        for i in range(c.NS):
            bk = (ch * c.NS + i) % 4
            pi, pb = bk // 2, bk % 2

            def mmo(e, slot=slot, i=i, pi=pi, pb=pb):
                last = None
                for kj in range(c.NJ):
                    last = e.matmul(
                        ps[pi][:, pb * 512: pb * 512 + 512],
                        m_tT[:, kj * c.SP + i * 128: kj * c.SP + (i + 1) * 128],
                        wslot(slot, kj),
                        start=(kj == 0), stop=(kj == c.NJ - 1))
                return last
            T.add("pe", mmo,
                  reads=[f"ring{slot}",
                         f"tT{(i * 2) // c.NS if c.NS > 1 else 0}"],
                  writes=[f"p{pi}b{pb}"], tag=f"o.m{i}_{ch}")
            T.add("dve", (lambda e, i=i, ch=ch, pi=pi, pb=pb:
                  e.tensor_tensor(
                      m_obf[:, i * c.D + ch * 512: i * c.D + (ch + 1) * 512],
                      ps[pi][:, pb * 512: pb * 512 + 512],
                      sb_bob[:, ch * 512:(ch + 1) * 512], ALU.add)),
                  reads=[f"p{pi}b{pb}", "bbc3"], writes=[f"obf{i}"],
                  tag=f"o.e{i}_{ch}")
    for i in range(c.NT):
        c_tail(i)

    for tc in range(c.NTC):
        hid3 = m_hid.ap().rearrange("p (g n) -> p g n", n=c.TC)

        def emit_h128(g, tc=tc):
            gp = g % 2
            T.add("pe", (lambda e, gp=gp: e.matmul(
                ps[2][:, gp * 512: gp * 512 + c.TC], sb_h128[:, :],
                m_hpre[gp][:, :], start=True, stop=True)),
                reads=["h128", f"hpre{gp}"], writes=[f"p2b{gp}"],
                tag=f"f{tc}.h{g}")
            T.add("act", (lambda e, g=g, gp=gp: e.copy(
                m_hid[:, g * c.TC:(g + 1) * c.TC],
                ps[2][:, gp * 512: gp * 512 + c.TC])),
                reads=[f"p2b{gp}"], writes=[f"hid{g}"], tag=f"f{tc}.he{g}")

        bfly_q = []

        def emit_ready_bfly(blk):
            o = 1
            while o < c.NGH:
                for k in range(c.NGH // (2 * o)):
                    if 4 * blk < (k + 1) * 2 * o <= 4 * (blk + 1):
                        bfly_q.append((o, k))
                o *= 2

        def drain_bfly(nmax):
            n = 0
            while bfly_q and n < nmax:
                o, k = bfly_q.pop(0)
                emit_bfly(o, k)
                n += 1

        def emit_bfly(o, k, tc=tc):
            a0 = 2 * o * k
            b0 = a0 + o
            keys = [f"hid{g}" for g in range(a0, a0 + 2 * o)]
            step = max(1, min(o, 16))
            for p0 in range(0, o, step):
                w = min(step, o - p0)
                A = hid3[:, a0 + p0:a0 + p0 + w, :]
                Bp = hid3[:, b0 + p0:b0 + p0 + w, :]
                T.add("dve", (lambda e, A=A, Bp=Bp: e.tensor_add(A, A, Bp)),
                      reads=keys, writes=keys, tag=f"f{tc}.ba{o}")
                T.add("dve", (lambda e, A=A, Bp=Bp: e.scalar_tensor_tensor(
                    Bp, Bp, -2.0, A, ALU.mult, ALU.add)),
                      reads=keys, writes=keys, tag=f"f{tc}.bb{o}")

        for blk in range(c.NGH // 4):
            s1 = ring_next()
            stream_wbf_chunk(w1T, blk, s1, f"f{tc}.w1_{blk}", q="pool")
            s3 = ring_next()
            stream_wbf_chunk(w3T, blk, s3, f"f{tc}.w3_{blk}", q="pool")
            for gg in range(4):
                g = blk * 4 + gg
                gp = g % 2

                def mmu(e, s1=s1, gg=gg, gp=gp, tc=tc):
                    last = None
                    for kj in range(c.NJ):
                        last = e.matmul(
                            ps[0][:, gp * 512: gp * 512 + c.TC],
                            wslot(s1, kj)[:, gg * 128:(gg + 1) * 128],
                            m_tT[:, kj * c.SP + tc * c.TC:
                                 kj * c.SP + (tc + 1) * c.TC],
                            start=(kj == 0), stop=(kj == c.NJ - 1))
                    return last
                T.add("pe", mmu,
                      reads=[f"ring{s1}", f"tT{tc if c.NTC > 1 else 0}"],
                      writes=[f"p0b{gp}"], tag=f"f{tc}.ug{g}")

                def mml(e, s3=s3, gg=gg, gp=gp, tc=tc):
                    last = None
                    for kj in range(c.NJ):
                        last = e.matmul(
                            ps[1][:, gp * 512: gp * 512 + c.TC],
                            wslot(s3, kj)[:, gg * 128:(gg + 1) * 128],
                            m_tT[:, kj * c.SP + tc * c.TC:
                                 kj * c.SP + (tc + 1) * c.TC],
                            start=(kj == 0), stop=(kj == c.NJ - 1))
                    return last
                T.add("pe", mml,
                      reads=[f"ring{s3}", f"tT{tc if c.NTC > 1 else 0}"],
                      writes=[f"p1b{gp}"], tag=f"f{tc}.ul{g}")
                T.add("act", (lambda e, gp=gp: e.activation(
                    m_sil[gp][:, :], ps[0][:, gp * 512: gp * 512 + c.TC],
                    AF.Silu)),
                    reads=[f"p0b{gp}"], writes=[f"sil{gp}"], tag=f"f{tc}.si{g}")
                T.add("dve", (lambda e, gp=gp: e.tensor_tensor(
                    m_hpre[gp][:, :], m_sil[gp][:, :],
                    ps[1][:, gp * 512: gp * 512 + c.TC], ALU.mult)),
                    reads=[f"sil{gp}", f"p1b{gp}"], writes=[f"hpre{gp}"],
                    tag=f"f{tc}.hm{g}")
                if g > 0:
                    emit_h128(g - 1)
                if g == c.NGH - 1:
                    emit_h128(g)
            if blk > 0:
                emit_ready_bfly(blk - 1)
                drain_bfly(10 ** 9)
        emit_ready_bfly(c.NGH // 4 - 1)
        drain_bfly(10 ** 9)
        for dt in range(c.NJ):
            slot = ring_next()
            src_r = w2T[:, dt * 128:(dt + 1) * 128].rearrange(
                "(g p) n -> p g n", p=128)
            dma("pool", m_ring[slot].ap().rearrange(
                "p (g n) -> p g n", n=128)[:, 0:c.NGH, :],
                src_r, f"rs{slot}", [], [f"ring{slot}"],
                tag=f"f{tc}.w2_{dt}")
            pb = dt % 2

            def mmd(e, slot=slot, pb=pb):
                last = None
                for g in range(c.NGH):
                    last = e.matmul(
                        ps[3][:, pb * 512: pb * 512 + c.TC],
                        m_ring[slot][:, g * 128:(g + 1) * 128],
                        m_hid[:, g * c.TC:(g + 1) * c.TC],
                        start=(g == 0), stop=(g == c.NGH - 1))
                return last
            T.add("pe", mmd,
                  reads=[f"ring{slot}"] + [f"hid{g}" for g in range(c.NGH)],
                  writes=[f"p3b{pb}"], tag=f"f{tc}.d{dt}")
            T.add("act", (lambda e, pb=pb: e.copy(
                m_dstg[pb][:, :], ps[3][:, pb * 512: pb * 512 + c.TC])),
                reads=[f"p3b{pb}"], writes=[f"dstg{pb}"], tag=f"f{tc}.de{dt}")
            transpose_batch(
                [m_outf[:, ts * c.D + dt * 128: ts * c.D + (dt + 1) * 128]
                 for ts in range(c.TCSUB)], "outf",
                [m_dstg[pb][:, ts * 128:(ts + 1) * 128]
                 for ts in range(c.TCSUB)], f"dstg{pb}",
                2, dt % 2, f"f{tc}.dtb{dt}", ev_eng="act")
        for ts in range(c.TCSUB):
            it = tc * c.TCSUB + ts
            dma("sp", m_yrb[:, :], y_dram[it * 128:(it + 1) * 128, :],
                "yrb", ["ydram"], ["yrb"], tag=f"f{tc}.yr{ts}")
            ln_stats(m_outf[:, ts * c.D:(ts + 1) * c.D], "outf", c.D,
                     f"f{tc}.lf{ts}", slot=ts % 2)
            ln_apply(m_ostg[:, :], "ostg",
                     m_outf[:, ts * c.D:(ts + 1) * c.D], "outf",
                     f"f{tc}.lf{ts}", slot=ts % 2)
            T.add("pool", (lambda e: e.tensor_tensor(
                m_ostg[:, :], m_ostg[:, :], m_yrb[:, :], ALU.add)),
                reads=["ostg", "yrb"], writes=["ostg"], tag=f"f{tc}.fa{ts}")
            dma("sp", out[it * 128:(it + 1) * 128, :], m_ostg[:, :],
                "outw", ["ostg"], ["outw"], tag=f"f{tc}.o{ts}")

    sem_names = set()
    for op in T.ops:
        for s, _ in op.waits:
            sem_names.add(s)
        for s, _ in op.incs:
            sem_names.add(s)
    sem_handles = {}

    import contextlib
    stack = contextlib.ExitStack()
    for s in sorted(sem_names):
        sem_handles[s] = stack.enter_context(nc.semaphore(s))

    out_total = T.dma_cum.get("ds_outw", 0)

    with stack:
        with nc.Block() as block:
            def emit_for(eng_name, eng_handle):
                for op in T.ops:
                    if op.eng != eng_name:
                        continue
                    for s, v in op.waits:
                        eng_handle.wait_ge(sem_handles[s], v)
                    inst = op.fn(eng_handle)
                    if inst is not None:
                        for s, v in op.incs:
                            if v is None:
                                inst.then_inc(sem_handles[s])
                            else:
                                inst.then_inc(sem_handles[s], v)
                if eng_name == "sp":
                    eng_handle.wait_ge(sem_handles["ds_outw"], out_total)

            @block.tensor
            def _(e):
                emit_for("pe", e)

            @block.vector
            def _(e):
                emit_for("dve", e)

            @block.scalar
            def _(e):
                emit_for("act", e)

            @block.gpsimd
            def _(e):
                emit_for("pool", e)

            @block.sync
            def _(e):
                emit_for("sp", e)

    return nc



def _hadamard(n):
    M = np.array([[1.0]], dtype=np.float64)
    while M.shape[0] < n:
        M = np.block([[M, M], [M, -M]])
    return M


def make_inputs_for_core(cfg, core, x, wq, bq, wk, bk, wv, bv, wo, bo,
                         W1, W2, W3, shared):
    c = cfg
    xr = x.reshape(-1, c.D)
    return {"x_all": np.ascontiguousarray(xr[core * c.SP:(core + 1) * c.SP]),
            **shared}


def make_shared(cfg, wq, bq, wk, bk, wv, bv, wo, bo, W1, W2, W3):
    c = cfg
    h128 = (_hadamard(128) / np.sqrt(c.HF)).astype(np.float32)
    consts = np.concatenate(
        [np.eye(128, dtype=np.float32), h128], axis=1).astype(bfloat16)
    return {
        "wqT": np.ascontiguousarray(wq.T),
        "wkT": np.ascontiguousarray(wk.T),
        "wvT": np.ascontiguousarray(wv.T),
        "woT": np.ascontiguousarray(wo.T),
        "w1T": np.ascontiguousarray(W1.T),
        "w3T": np.ascontiguousarray(W3.T),
        "w2T": np.ascontiguousarray(W2.T),
        "biases": np.ascontiguousarray(np.stack([bq, bk, bv, bo])),
        "consts": consts,
    }


def kernel(L0_attn_hidden_states, L0_attn_wq, L0_attn_bq, L0_attn_wk,
           L0_attn_bk, L0_attn_wv, L0_attn_bv, L0_attn_wo, L0_attn_bo,
           L0_ffn_W1, L0_ffn_W2, L0_ffn_W3):
    x = np.asarray(L0_attn_hidden_states, dtype=np.float32)
    B, S, D = x.shape
    cfg = Cfg(S_total=B * S, D=D, HF=L0_ffn_W1.shape[0],
              NH=16, n_cores=8)
    nc = build_program(cfg)
    shared = make_shared(cfg, L0_attn_wq, L0_attn_bq, L0_attn_wk, L0_attn_bk,
                         L0_attn_wv, L0_attn_bv, L0_attn_wo, L0_attn_bo,
                         L0_ffn_W1, L0_ffn_W2, L0_ffn_W3)
    shared = {k: np.ascontiguousarray(np.asarray(v, dtype=(
        bfloat16 if k == "consts" else np.float32))) for k, v in shared.items()}
    in_maps = [make_inputs_for_core(cfg, core, x, L0_attn_wq, L0_attn_bq,
                                    L0_attn_wk, L0_attn_bk, L0_attn_wv,
                                    L0_attn_bv, L0_attn_wo, L0_attn_bo,
                                    L0_ffn_W1, L0_ffn_W2, L0_ffn_W3, shared)
               for core in range(cfg.n_cores)]
    res = run_bass_kernel_spmd(nc, in_maps, core_ids=list(range(cfg.n_cores)))
    parts = [np.asarray(res.results[i]["out"]) for i in range(cfg.n_cores)]
    return np.concatenate(parts, axis=0).reshape(B, S, D).astype(np.float32)
